# version 02 (final)
# baseline (speedup 1.0000x reference)
import numpy as np
import ml_dtypes
import concourse.bass as bass
import concourse.mybir as mybir
from concourse.bass_utils import run_bass_kernel_spmd

F32 = mybir.dt.float32
BF16 = mybir.dt.bfloat16
AF = mybir.ActivationFunctionType
OP = mybir.AluOpType

B, S, D, H = 8, 2048, 2048, 8192
KC = D // 128
RC = KC // 2
TH = 1024
NTH = S // TH
HLOC = H // 8
HC = HLOC // 128
EPS1 = 1e-5
EPS2 = 1e-6

DBG = ()
STOP = None



def _stripes(w):
    din, eout = w.shape
    kc, ec = din // 128, eout // 128
    return np.ascontiguousarray(
        w.reshape(kc, 128, ec, 128).transpose(2, 1, 0, 3).reshape(ec, 128, kc * 128),
        dtype=np.float32)


def _cols(v):
    v = np.asarray(v, np.float32)
    return np.ascontiguousarray(v.reshape(-1, 128).T)


def _rope_tables():
    rot = D // 2
    inv = 1.0 / (10000.0 ** (np.arange(0, rot, 2, dtype=np.float64) / rot))
    ang = np.arange(S, dtype=np.float64)[None, :] * np.repeat(inv, 2)[:, None]
    cos = np.cos(ang).astype(np.float32)
    sin = np.sin(ang).astype(np.float32)
    sgn = np.where((np.arange(rot) % 2) == 0, -1.0, 1.0)[:, None].astype(np.float32)
    sins = sin * sgn
    tbl = np.empty((RC, 128, 2 * S), np.float32)
    for rc in range(RC):
        tbl[rc, :, :S] = cos[rc * 128:(rc + 1) * 128]
        tbl[rc, :, S:] = sins[rc * 128:(rc + 1) * 128]
    return tbl.astype(ml_dtypes.bfloat16)


def host_prep(inputs):
    X = np.asarray(inputs["L0_attn_X"], np.float32)
    W_in = np.asarray(inputs["L0_attn_W_in"], np.float32)
    b_in = np.asarray(inputs["L0_attn_b_in"], np.float32)
    W_out = np.asarray(inputs["L0_attn_W_out"], np.float32)
    b_out = np.asarray(inputs["L0_attn_b_out"], np.float32)
    agg_w = np.asarray(inputs["L0_ffn_agg_w"], np.float32)
    d1_w = np.asarray(inputs["L0_ffn_d1_w"], np.float32)
    d2_w = np.asarray(inputs["L0_ffn_d2_w"], np.float32)
    bc_w = np.asarray(inputs["L0_ffn_bc_w"], np.float32)
    gate_w = np.asarray(inputs["L0_ffn_gate_w"], np.float32)

    wqkv = _stripes(W_in.T)
    wout = _stripes(W_out.T)
    wgate = _stripes(gate_w)
    wagg = _stripes(agg_w)
    wbc = _stripes(bc_w)

    vc = np.concatenate([
        _cols(b_in),
        _cols(b_out),
        _cols(np.asarray(inputs["L0_ffn_gate_b"])),
        _cols(np.asarray(inputs["L0_ffn_agg_b"])),
        _cols(np.asarray(inputs["L0_ffn_bc_b"])),
        _cols(np.asarray(inputs["L0_ffn_pn_gamma"])),
        _cols(np.asarray(inputs["L0_ffn_pn_beta"])),
        _cols(np.asarray(inputs["L0_ffn_out_gamma"])),
        _cols(np.asarray(inputs["L0_ffn_out_beta"])),
        _cols(np.asarray(inputs["L0_ffn_in_gamma"])),
        _cols(np.asarray(inputs["L0_ffn_in_beta"])),
    ], axis=1)

    def _rep_cmaj(v):
        return np.ascontiguousarray(
            np.repeat(_cols(v)[:, :, None], 8, axis=2).reshape(128, 128))

    d2b_rep = _rep_cmaj(np.asarray(inputs["L0_ffn_d2_b"])).astype(np.float32)
    png_rep = _rep_cmaj(np.asarray(inputs["L0_ffn_pn_gamma"])).astype(ml_dtypes.bfloat16)
    pnb_rep = _rep_cmaj(np.asarray(inputs["L0_ffn_pn_beta"])).astype(ml_dtypes.bfloat16)

    tbl = _rope_tables()
    ident = np.eye(128, dtype=ml_dtypes.bfloat16)

    in_maps = []
    for c in range(8):
        sel = np.zeros((8,), np.float32); sel[c] = 1.0
        selm = np.tile(sel[None, None, :], (128, KC, 1)).reshape(128, 128)
        d1s = _stripes(d1_w[:, c * HLOC:(c + 1) * HLOC])
        d2s = _stripes(d2_w[c * HLOC:(c + 1) * HLOC, :])
        d1b = np.zeros((128, 8), np.float32)
        d1b[:, :] = _cols(np.asarray(inputs["L0_ffn_d1_b"], np.float32)
                          [c * HLOC:(c + 1) * HLOC])
        in_maps.append({
            "xt": np.ascontiguousarray(X[c].T),
            "wqkv": wqkv, "wout": wout, "wgate": wgate, "wagg": wagg,
            "wbc": wbc, "wd1": d1s, "wd2": d2s,
            "vc": vc, "xlastc": _cols(X[c][-1]), "d2brep": d2b_rep,
            "pngrep": png_rep, "pnbrep": pnb_rep, "d1b": d1b,
            "tbl": tbl, "ident": ident, "selm": np.ascontiguousarray(selm),
        })
    return in_maps



class Sem:
    def __init__(self, nc, es, name):
        self.h = es.enter_context(nc.semaphore(name))
        self.n = 0

    def inc(self, k=1):
        self.n += k
        return self.n


class Prog:
    ENGS = ("tensor", "vector", "scalar", "gpsimd", "sync")

    def __init__(self):
        self.q = {k: [] for k in self.ENGS}

    def emit(self, eng, fn):
        self.q[eng].append(fn)


def wait(sem, val):
    assert val <= sem.n, f"wait beyond emitted incs: {val} > {sem.n}"
    return lambda e, s=sem, v=val: e.wait_ge(s.h, v)


RQ = 512


def _q(ap, lo, n):
    return ap[0:1, lo:lo + n] if ap.shape[0] == 1 else ap[:, lo:lo + n]


def rtt(eng, out, in0, in1, op, width=2048):
    last = None
    for lo in range(0, width, RQ):
        last = eng.tensor_tensor(_q(out, lo, RQ), _q(in0, lo, RQ), _q(in1, lo, RQ), op)
    return last


def rts(eng, out, in0, s1, s2, op0, op1=None, width=2048):
    last = None
    for lo in range(0, width, RQ):
        if op1 is None:
            last = eng.tensor_scalar(_q(out, lo, RQ), _q(in0, lo, RQ), s1, None, op0)
        else:
            last = eng.tensor_scalar(_q(out, lo, RQ), _q(in0, lo, RQ), s1, s2, op0, op1)
    return last


def rcopy(eng, out, in_, width=2048):
    last = None
    for lo in range(0, width, RQ):
        last = eng.tensor_copy(_q(out, lo, RQ), _q(in_, lo, RQ))
    return last


def rrecip(eng, out, in_, width=2048):
    last = None
    for lo in range(0, width, RQ):
        last = eng.reciprocal(_q(out, lo, RQ), _q(in_, lo, RQ))
    return last


def rreduce(eng, out11, in_, scratch4, op, width=2048):
    for i, lo in enumerate(range(0, width, RQ)):
        eng.tensor_reduce(scratch4[0:1, i:i + 1], _q(in_, lo, RQ),
                          mybir.AxisListType.X, op)
    eng.tensor_tensor(scratch4[0:1, 0:2], scratch4[0:1, 0:2], scratch4[0:1, 2:4], op)
    return eng.tensor_tensor(out11, scratch4[0:1, 0:1], scratch4[0:1, 1:2], op)


class Overlay:

    def __init__(self, nc, es, total_bytes):
        self.nc = nc
        self.total = total_bytes
        self.handle = es.enter_context(
            nc.sbuf_tensor("ARENA", [128, total_bytes], mybir.dt.uint8))
        self.base = nc.lookup_mloc(self.handle).addr
        self.tensors = []

    def at(self, off, name, shape, dtype):
        per_part = int(np.prod(shape[1:])) * mybir.dt.size(dtype)
        assert off + per_part <= self.total, (name, off, per_part, self.total)
        assert off % 32 == 0, (name, off)
        hnd = self.nc.virtual_sbuf_tensor(name, list(shape), dtype)
        self.nc._tensor(hnd.name, list(hnd.shape), hnd.dtype, type="SB")
        mloc = self.nc.lookup_mloc(hnd)
        mloc.allocated = True
        mloc.addr = self.base + off
        self.tensors.append((name, off, per_part))
        return hnd



def build_nc():
    nc = bass.Bass("TRN2", target_bir_lowering=False, debug=False)

    xt_h = nc.dram_tensor("xt", [D, S], F32, kind="ExternalInput").ap()
    wqkv_h = nc.dram_tensor("wqkv", [48, 128, KC * 128], F32, kind="ExternalInput").ap()
    wout_h = nc.dram_tensor("wout", [16, 128, KC * 128], F32, kind="ExternalInput").ap()
    wgate_h = nc.dram_tensor("wgate", [16, 128, KC * 128], F32, kind="ExternalInput").ap()
    wagg_h = nc.dram_tensor("wagg", [16, 128, KC * 128], F32, kind="ExternalInput").ap()
    wbc_h = nc.dram_tensor("wbc", [16, 128, KC * 128], F32, kind="ExternalInput").ap()
    wd1_h = nc.dram_tensor("wd1", [HC, 128, KC * 128], F32, kind="ExternalInput").ap()
    wd2_h = nc.dram_tensor("wd2", [KC, 128, HC * 128], F32, kind="ExternalInput").ap()
    vc_h = nc.dram_tensor("vc", [128, 208], F32, kind="ExternalInput").ap()
    xlastc_h = nc.dram_tensor("xlastc", [128, 16], F32, kind="ExternalInput").ap()
    d2brep_h = nc.dram_tensor("d2brep", [128, 128], F32, kind="ExternalInput").ap()
    pngrep_h = nc.dram_tensor("pngrep", [128, 128], BF16, kind="ExternalInput").ap()
    pnbrep_h = nc.dram_tensor("pnbrep", [128, 128], BF16, kind="ExternalInput").ap()
    d1b_h = nc.dram_tensor("d1b", [128, 8], F32, kind="ExternalInput").ap()
    tbl_h = nc.dram_tensor("tbl", [RC, 128, 2 * S], BF16, kind="ExternalInput").ap()
    ident_h = nc.dram_tensor("ident", [128, 128], BF16, kind="ExternalInput").ap()
    selm_h = nc.dram_tensor("selm", [128, 128], F32, kind="ExternalInput").ap()

    yt_h = nc.dram_tensor("yt", [D, S], F32, kind="ExternalOutput").ap()

    x1scr = nc.dram_tensor("x1scr", [D, S], F32, kind="Internal").ap()
    agin = nc.dram_tensor("agin", [128, 16], BF16, kind="Internal").ap()
    agout = nc.dram_tensor("agout", [8 * 128, 16], BF16, kind="Internal").ap()
    arin = [nc.dram_tensor(f"arin{i}", [128, 128], F32, kind="Internal").ap() for i in range(4)]
    arout = [nc.dram_tensor(f"arout{i}", [128, 128], F32, kind="Internal").ap() for i in range(4)]

    dbg = {}
    if DBG:
        if "xn" in DBG: dbg["xn"] = nc.dram_tensor("dbg_xn", [128, KC * S], F32, kind="ExternalOutput").ap()
        if "k" in DBG: dbg["k"] = nc.dram_tensor("dbg_k", [128, KC * S], F32, kind="ExternalOutput").ap()
        if "q" in DBG: dbg["q"] = nc.dram_tensor("dbg_q", [128, KC * S], F32, kind="ExternalOutput").ap()
        if "a" in DBG: dbg["a"] = nc.dram_tensor("dbg_a", [128, KC], F32, kind="ExternalOutput").ap()
        if "x1" in DBG: dbg["x1"] = nc.dram_tensor("dbg_x1", [128, KC * S], F32, kind="ExternalOutput").ap()
        if "xnl" in DBG: dbg["xnl"] = nc.dram_tensor("dbg_xnl", [128, 16], F32, kind="ExternalOutput").ap()
        if "rows" in DBG: dbg["rows"] = nc.dram_tensor("dbg_rows", [2, 128, S], F32, kind="ExternalOutput").ap()
        if "ch0" in DBG or "tm0" in DBG or "tm1" in DBG: dbg["ch0"] = nc.dram_tensor("dbg_ch0", [4, 128, S], F32, kind="ExternalOutput").ap()
        if "th" in DBG: dbg["th"] = nc.dram_tensor("dbg_th", [5, 128, S], F32, kind="ExternalOutput").ap()
        if "bc" in DBG: dbg["bc"] = nc.dram_tensor("dbg_bc", [128, KC], F32, kind="ExternalOutput").ap()
        if "out" in DBG: dbg["out"] = nc.dram_tensor("dbg_out", [128, KC * S], F32, kind="ExternalOutput").ap()

    from contextlib import ExitStack
    es = ExitStack()

    bufB = es.enter_context(nc.sbuf_tensor("bufB", [128, KC * S], BF16))
    bufC = es.enter_context(nc.sbuf_tensor("bufC", [128, KC * S], BF16))
    wbuf = [es.enter_context(nc.sbuf_tensor(f"wbuf{i}", [128, KC * 128], BF16)) for i in range(2)]
    cbuf = [es.enter_context(nc.sbuf_tensor(f"cbuf{i}", [128, KC * 128], BF16)) for i in range(2)]
    mu1b = es.enter_context(nc.sbuf_tensor("mu1b", [128, S], BF16))
    r1b = es.enter_context(nc.sbuf_tensor("r1b", [128, S], BF16))
    RW0 = es.enter_context(nc.sbuf_tensor("RW0", [128, S], F32))
    RW1 = es.enter_context(nc.sbuf_tensor("RW1", [128, S], F32))

    VC = es.enter_context(nc.sbuf_tensor("VC", [128, 208], F32))
    scb = es.enter_context(nc.sbuf_tensor("scb", [1, 32], BF16))
    CWB = es.enter_context(nc.sbuf_tensor("CWB", [128, 16], BF16))
    d2brep = es.enter_context(nc.sbuf_tensor("d2brepc", [128, 128], BF16))
    d1bcol = es.enter_context(nc.sbuf_tensor("d1bcol", [128, 8], F32))
    selm = es.enter_context(nc.sbuf_tensor("selmc", [128, 128], BF16))
    Apart = es.enter_context(nc.sbuf_tensor("Apart", [128, 2 * KC], F32))
    Acol = es.enter_context(nc.sbuf_tensor("Acol", [128, KC], F32))
    bAcol = es.enter_context(nc.sbuf_tensor("bAcol", [128, KC], F32))
    ylastc = es.enter_context(nc.sbuf_tensor("ylastc", [128, KC], BF16))
    thfm = es.enter_context(nc.sbuf_tensor("thfm", [128, 128], F32))
    tnfm = es.enter_context(nc.sbuf_tensor("tnfm", [128, 128], BF16))
    ufm = es.enter_context(nc.sbuf_tensor("ufm", [128, 8 * 8], BF16))
    dlt = es.enter_context(nc.sbuf_tensor("dlt", [128, 128], F32))
    dsum = dlt
    bcall = es.enter_context(nc.sbuf_tensor("bcall", [128, 128], BF16))
    bcown = es.enter_context(nc.sbuf_tensor("bcown", [128, KC], F32))
    ones_c = es.enter_context(nc.sbuf_tensor("ones_c", [128, 1], BF16))
    ones_r = es.enter_context(nc.sbuf_tensor("ones_r", [1, 128], BF16))
    CT = es.enter_context(nc.sbuf_tensor("CT", [128, 8], F32))
    SC8 = es.enter_context(nc.sbuf_tensor("SC8", [16, 32], F32))
    SSC = SC8
    rowbf = es.enter_context(nc.sbuf_tensor("rowbf", [1, S], BF16))

    OV = Overlay(nc, es, 16384 + 16384)
    tblb = [OV.at(0, "tblA", [128, 2 * S], BF16), OV.at(8192, "tblB", [128, 2 * S], BF16)]
    invnb = OV.at(0, "invnb", [128, S], BF16)
    x1tmp = OV.at(0, "x1tmp", [128, TH * 2], F32)
    CW = OV.at(8192, "CW", [128, 96], F32)
    sqt = OV.at(8704, "sqt", [128, 128], BF16)
    pngrep = OV.at(9216, "pngrepc", [128, 128], BF16)
    pnbrep = OV.at(9728, "pnbrepc", [128, 128], BF16)
    stb8 = OV.at(10752, "stb8", [128, 32], F32)
    x1in = [OV.at(0, "x1inA", [128, TH], F32), OV.at(4096, "x1inB", [128, TH], F32)]
    x2out = [OV.at(8192, "x2oA", [128, TH], F32), OV.at(12288, "x2oB", [128, TH], F32)]
    RC0 = 16384
    fstream = [OV.at(RC0, "fsA", [128, S], F32), OV.at(RC0 + 8192, "fsB", [128, S], F32)]
    shtmp = OV.at(RC0, "shtmp", [128, TH], BF16)
    p1tmp = OV.at(RC0 + 2048, "p1tmp", [128, TH], BF16)
    vtmp = [OV.at(RC0 + 4096, "vtA", [128, TH], BF16), OV.at(RC0 + 6144, "vtB", [128, TH], BF16)]
    prodt = OV.at(RC0 + 8192, "prodt", [128, TH], BF16)
    ksqt = [OV.at(RC0 + 10240, "ksA", [128, TH], BF16), OV.at(RC0 + 12288, "ksB", [128, TH], BF16)]
    sigt = [OV.at(RC0 + 4096, "sgA", [128, TH], BF16), OV.at(RC0 + 6144, "sgB", [128, TH], BF16)]
    xsqt = [OV.at(RC0 + 10240, "xqA", [128, TH], BF16), OV.at(RC0 + 12288, "xqB", [128, TH], BF16)]
    m2b = OV.at(RC0, "m2b", [128, S], BF16)
    r2b = OV.at(RC0 + 4096, "r2b", [128, S], BF16)
    m3b = OV.at(RC0, "m3b", [128, S], BF16)
    r3b = OV.at(RC0 + 4096, "r3b", [128, S], BF16)

    pM = es.enter_context(nc.psum_tensor("pM", [128, 2 * TH], F32))
    pC = es.enter_context(nc.psum_tensor("pC", [128, 512], F32))
    pT = es.enter_context(nc.psum_tensor("pT", [128, 1024], BF16))
    pRow = es.enter_context(nc.psum_tensor("pRow", [128, 1024], F32))

    _SEMS = []

    class _SemT(Sem):
        def __init__(self, nc_, es_, name):
            super().__init__(nc_, es_, name)
            _SEMS.append(self)
    Sem_ = _SemT
    sBulk = Sem_(nc, es, "sBulk")
    sWB = [Sem_(nc, es, f"sWB{i}") for i in range(2)]
    sCB = [Sem_(nc, es, f"sCB{i}") for i in range(2)]
    sTB = [Sem_(nc, es, f"sTB{i}") for i in range(2)]
    sFS = [Sem_(nc, es, f"sFS{i}") for i in range(2)]
    sSY = Sem_(nc, es, "sSY")
    sSCR = Sem_(nc, es, "sSCR")
    sOUT = Sem_(nc, es, "sOUT")
    sST = Sem_(nc, es, "sST")
    sCC = Sem_(nc, es, "sCC")
    sX1T = [Sem_(nc, es, f"sX1T{i}") for i in range(2)]
    sXI = [Sem_(nc, es, f"sXI{i}") for i in range(2)]
    sT = Sem_(nc, es, "sT")
    sV = Sem_(nc, es, "sV")
    sS = Sem_(nc, es, "sS")
    sG = Sem_(nc, es, "sG")

    _nums = sorted(s.h.num for s in _SEMS)
    assert _nums == list(range(_nums[0], _nums[0] + len(_nums))), _nums
    nc.gpsimd.dma_reset(range(_nums[0], _nums[-1] + 1))
    nc.gpsimd.sem_clear(range(_nums[0], _nums[-1] + 1))
    nc.all_engine_barrier()

    P = Prog()

    R_S1, R_MU, R_M2, R_SD, R_S2, R_RS, R_E2, R_SP = range(8)
    R_INV = R_E2

    def row(i):
        t = RW0 if i < 4 else RW1
        p = (i % 4) * 32
        return t[p:p + 1, :]

    def f_bulk(g):
        g.dma_start(out=VC[:, :], in_=vc_h[:, :]).then_inc(sBulk.h, 16)
        g.dma_start(out=d2brep[:, :], in_=d2brep_h[:, :]).then_inc(sBulk.h, 16)
        g.dma_start(out=d1bcol[:, :], in_=d1b_h[:, :]).then_inc(sBulk.h, 16)
        g.dma_start(out=selm[:, :], in_=selm_h[:, :]).then_inc(sBulk.h, 16)
        g.dma_start(out=bufB.ap().rearrange("p (c s) -> p c s", c=KC),
                    in_=xt_h.rearrange("(c p) s -> p c s", p=128)).then_inc(sBulk.h, 16)
    P.emit("gpsimd", f_bulk)
    sBulk.inc(5 * 16)
    BULK_ALL = sBulk.n

    def f_v0(v):
        v.memset(ones_c[:, :], 1.0)
        v.memset(ones_r[:, :], 1.0)
        v.memset(CT[:, 0:1], EPS1)
        v.memset(CT[:, 1:2], EPS2)
        v.memset(CT[:, 2:3], 1e-12)
        v.memset(CT[:, 3:4], 0.0).then_inc(sV.h, 1)
    P.emit("vector", f_v0)
    V_CONST = sV.inc()


    def emit_stats_pass(src_fn, sq_bufs, do_sum, sq_first=(), mm_first=()):
        tmm = {}
        evs = []
        for tq in range(4):
            for c in range(KC):
                idx = tq * KC + c

                def f_sq(s, tq=tq, c=c, idx=idx,
                         wf=tuple(sq_first) if idx == 0 else (),
                         wt=wait(sT, tmm[idx - 2]) if idx >= 2 else None):
                    for w in wf:
                        w(s)
                    if wt is not None:
                        wt(s)
                    s.activation(sq_bufs[idx % 2][:, 0:512], src_fn(c, tq),
                                 AF.Square).then_inc(sS.h, 1)
                P.emit("scalar", f_sq)
                nsq = sS.inc()

                def f_mm(t, tq=tq, c=c, idx=idx, nsq=nsq,
                         we=(wait(sV, evs[-1]) if (c == 0 and tq > 0) else None),
                         wx=tuple(mm_first) if idx == 0 else ()):
                    if we is not None:
                        we(t)
                    for w in wx:
                        w(t)
                    if do_sum:
                        t.matmul(pRow[0:1, 0:512], ones_c[:, :], src_fn(c, tq),
                                 start=(c == 0), stop=(c == KC - 1))
                    t.wait_ge(sS.h, nsq)
                    t.matmul(pRow[0:1, 512:1024], ones_c[:, :],
                             sq_bufs[idx % 2][:, 0:512],
                             start=(c == 0), stop=(c == KC - 1)).then_inc(sT.h, 1)
                P.emit("tensor", f_mm)
                tmm[idx] = sT.inc()

            def f_ev(v, tq=tq, w=wait(sT, tmm[tq * KC + KC - 1]), do_sum=do_sum):
                w(v)
                if do_sum:
                    v.tensor_copy(row(R_S1)[0:1, tq * 512:(tq + 1) * 512],
                                  pRow[0:1, 0:512])
                v.tensor_copy(row(R_S2)[0:1, tq * 512:(tq + 1) * 512],
                              pRow[0:1, 512:1024]).then_inc(sV.h, 1)
            P.emit("vector", f_ev)
            evs.append(sV.inc())
        return evs

    def xb(c, lo, n):
        return bufB[:, c * S + lo: c * S + lo + n]

    ln1_ev = emit_stats_pass(lambda c, tq: xb(c, tq * 512, 512), ksqt, True,
                             sq_first=[wait(sBulk, BULK_ALL)],
                             mm_first=[wait(sV, V_CONST), wait(sBulk, BULK_ALL)])
    def emit_ln_rows(eng_list, eps_col, mu_slot, rs_slot, after_sv):
        assert mu_slot == R_MU and rs_slot == R_RS

        def f_rows_v1(v, w=wait(sV, after_sv)):
            w(v)
            rts(v, row(R_MU), row(R_S1), 1.0 / D, None, OP.mult)
            rts(v, row(R_E2), row(R_S2), 1.0 / D, None, OP.mult)
            rtt(v, row(R_M2), row(R_MU), row(R_MU), OP.mult)
            rtt(v, row(R_E2), row(R_E2), row(R_M2), OP.subtract).then_inc(sV.h, 1)
        P.emit("vector", f_rows_v1)
        n1 = sV.inc()

        def f_rows_s(s, w=wait(sV, n1)):
            w(s)
            last = None
            for q in range(4):
                last = s.activation(row(R_M2)[0:1, q * 512:(q + 1) * 512],
                                    row(R_E2)[0:1, q * 512:(q + 1) * 512], AF.Sqrt,
                                    bias=CT[64:65, eps_col:eps_col + 1], scale=1.0)
            last.then_inc(sS.h, 1)
        P.emit("scalar", f_rows_s)
        n2 = sS.inc()

        def f_rows_v2(v, w=wait(sS, n2)):
            w(v)
            rrecip(v, row(R_RS), row(R_M2)).then_inc(sV.h, 1)
        P.emit("vector", f_rows_v2)
        return sV.inc()

    ln1_rows = emit_ln_rows(None, 0, R_MU, R_RS, ln1_ev[-1])

    def emit_bcast(dst, src_slot, after_sv, tag):
        def f_c(v, w=wait(sV, after_sv), slot=src_slot, tag=tag):
            w(v)
            rcopy(v, rowbf[0:1, 0:S], row(slot)).then_inc(sV.h, 1)
        P.emit("vector", f_c)
        nv = sV.inc()

        def f_t1(t, w=wait(sV, nv)):
            w(t)
            t.matmul(pRow[:, 0:512], ones_r[0:1, :], rowbf[0:1, 0:512],
                     start=True, stop=True)
            t.matmul(pRow[:, 512:1024], ones_r[0:1, :], rowbf[0:1, 512:1024],
                     start=True, stop=True).then_inc(sT.h, 1)
        P.emit("tensor", f_t1)
        g1 = sT.inc()

        def f_e1(v, w=wait(sT, g1)):
            w(v)
            v.tensor_copy(dst[:, 0:512], pRow[:, 0:512])
            v.tensor_copy(dst[:, 512:1024], pRow[:, 512:1024]).then_inc(sV.h, 1)
        P.emit("vector", f_e1)
        e1 = sV.inc()

        def f_t2(t, w=wait(sV, e1)):
            w(t)
            t.matmul(pRow[:, 0:512], ones_r[0:1, :], rowbf[0:1, 1024:1536],
                     start=True, stop=True)
            t.matmul(pRow[:, 512:1024], ones_r[0:1, :], rowbf[0:1, 1536:2048],
                     start=True, stop=True).then_inc(sT.h, 1)
        P.emit("tensor", f_t2)
        g2 = sT.inc()

        def f_e2(v, w=wait(sT, g2)):
            w(v)
            v.tensor_copy(dst[:, 1024:1536], pRow[:, 0:512])
            v.tensor_copy(dst[:, 1536:2048], pRow[:, 512:1024]).then_inc(sV.h, 1)
        P.emit("vector", f_e2)
        return sV.inc()

    ln1_bc = emit_bcast(mu1b, R_MU, ln1_rows, "mu1")
    ln1_bc = emit_bcast(r1b, R_RS, ln1_bc, "r1")

    xn_done = []
    for c in range(KC):
        def f_xn(v, c=c, w=wait(sV, ln1_bc)):
            if c == 0:
                w(v)
            v.tensor_sub(bufB[:, c * S:(c + 1) * S], bufB[:, c * S:(c + 1) * S],
                         mu1b[:, :])
            v.tensor_mul(bufB[:, c * S:(c + 1) * S], bufB[:, c * S:(c + 1) * S],
                         r1b[:, :]).then_inc(sV.h, 1)
        P.emit("vector", f_xn)
        xn_done.append(sV.inc())
    XN_ALL = xn_done[-1]

    if "xn" in dbg:
        def f_dxn(g, w=wait(sV, XN_ALL)):
            w(g)
            g.dma_start(out=dbg["xn"][:, :], in_=bufB[:, :]).then_inc(sBulk.h, 16)
        P.emit("gpsimd", f_dxn)
        sBulk.inc(16)

    wbuf_users = [0, 0]
    wbuf_loads = [0, 0]

    def stripe_load(src_ap, buf_i, free_at_sT):
        sWB[buf_i].inc(16)
        tgt = sWB[buf_i].n

        def f_w(g, src=src_ap, bi=buf_i, need=free_at_sT):
            if need > 0:
                g.wait_ge(sT.h, need)
            g.dma_start(out=wbuf[bi][:, :], in_=src).then_inc(sWB[bi].h, 16)
        P.emit("gpsimd", f_w)
        return tgt

    def mm_group(ptile_off, wb_i, wload_tgt, rhs_fn, nk, extra_wait=None,
                 psum=None, width=TH):
        psum = psum if psum is not None else pM

        def f_t(t, off=ptile_off, bi=wb_i, wn=wload_tgt, rhs=rhs_fn, nk=nk,
                ew=extra_wait, w=width):
            t.wait_ge(sWB[bi].h, wn)
            if ew is not None:
                ew(t)
            last = None
            for k in range(nk):
                for h in range(w // 512):
                    last = t.matmul(psum[:, off + h * 512: off + h * 512 + 512],
                                    wbuf[bi][:, k * 128:(k + 1) * 128],
                                    rhs(k, h),
                                    start=(k == 0), stop=(k == nk - 1))
            last.then_inc(sT.h, 1)
        P.emit("tensor", f_t)
        return sT.inc()

    def xnrhs(th):
        def rhs(k, h):
            lo = k * S + th * TH + h * 512
            return bufB[:, lo: lo + 512]
        return rhs

    def tbl_load(rc_i, buf_i, free_at_sV):
        sTB[buf_i].inc(16)
        tgt = sTB[buf_i].n

        def f_tb(g, rc=rc_i, bi=buf_i, need=free_at_sV):
            if need > 0:
                g.wait_ge(sV.h, need)
            g.dma_start(out=tblb[bi][:, :], in_=tbl_h[rc, :, :]).then_inc(sTB[bi].h, 16)
        P.emit("gpsimd", f_tb)
        return tgt

    ep_idx = 0
    last_ep = {}

    tbl_tgt = {}
    tbl_free_v = {0: 0, 1: 0}

    qkv_order = list(range(16, 48)) + list(range(0, 16))
    stripe_tgts = {}
    tbl_plan = {}
    for i, ec in enumerate(qkv_order):
        if 16 <= ec < 24:
            tbl_plan[ec] = ec - 16
        elif ec < 8:
            tbl_plan[ec] = ec

    def p1_epilogue(ec, th, ptile, gcount, tbl_i, tbl_tgt_n):
        nonlocal ep_idx
        bcol = VC[:, ec:ec + 1]
        if 16 <= ec < 32:
            c = ec - 16
            dst = bufC[:, c * S + th * TH: c * S + (th + 1) * TH]

            def f_s(s, w=wait(sT, gcount), dst=dst, pt=ptile, bc=bcol):
                w(s)
                s.activation(dst, pt, AF.Identity, bias=bc,
                             scale=1.0).then_inc(sS.h, 1)
            P.emit("scalar", f_s)
            ns = sS.inc()

            if c < 8:
                def f_v(v, w=wait(sS, ns), wt=wait(sTB[tbl_i], tbl_tgt_n),
                        dst=dst, th=th, ti=tbl_i):
                    w(v); wt(v)
                    v.stream_shuffle(shtmp[:, :], dst, [i ^ 1 for i in range(32)])
                    v.tensor_mul(p1tmp[:, :], dst, tblb[ti][:, th * TH: (th + 1) * TH])
                    v.tensor_mul(shtmp[:, :], shtmp[:, :],
                                 tblb[ti][:, S + th * TH: S + (th + 1) * TH])
                    v.tensor_add(dst, p1tmp[:, :], shtmp[:, :]).then_inc(sV.h, 1)
                P.emit("vector", f_v)
                nv = sV.inc()
                tbl_free_v[tbl_i] = nv
                return None, ns, nv
            else:
                def f_v(v, w=wait(sS, ns)):
                    w(v)
                    v.engine_nop().then_inc(sV.h, 1)
                P.emit("vector", f_v)
                return None, ns, sV.inc()

        if 32 <= ec < 48:
            c = ec - 32
            slot = (c * 2 + th) % 2
            wprev = (wait(sV, vtmp_user[slot]) if vtmp_user[slot] else None)

            def f_s(s, w=wait(sT, gcount), pt=ptile, bc=bcol, c=c, th=th,
                    wp=wprev, slot=slot):
                w(s)
                if wp is not None:
                    wp(s)
                s.activation(vtmp[slot][:, :], pt, AF.Identity,
                             bias=bc, scale=1.0).then_inc(sS.h, 1)
            P.emit("scalar", f_s)
            ns = sS.inc()

            def f_v(v, w=wait(sS, ns), winv=wait(sV, INV_BC[0]) if INV_BC else None,
                    c=c, th=th, slot=slot):
                w(v)
                if winv is not None:
                    winv(v)
                vt = vtmp[slot]
                v.tensor_mul(vt[:, :], vt[:, :], invnb[:, th * TH:(th + 1) * TH])
                v.tensor_mul(prodt[:, :], vt[:, :],
                             bufC[:, c * S + th * TH: c * S + (th + 1) * TH])
                v.tensor_reduce(Apart[:, c * 2 + th: c * 2 + th + 1], prodt[:, :],
                                mybir.AxisListType.X, OP.add).then_inc(sV.h, 1)
            P.emit("vector", f_v)
            nv = sV.inc()
            vtmp_user[slot] = nv
            return None, ns, nv

        c = ec
        dst = bufC[:, c * S + th * TH: c * S + (th + 1) * TH]
        wA = wait(sV, A_HOLD[0])
        if c < 8:
            def f_s(s, w=wait(sT, gcount), dst=dst, pt=ptile, bc=bcol):
                w(s)
                s.activation(dst, pt, AF.Identity, bias=bc, scale=1.0).then_inc(sS.h, 1)
            P.emit("scalar", f_s)
            ns = sS.inc()

            def f_v(v, w=wait(sS, ns), wt=wait(sTB[tbl_i], tbl_tgt_n),
                    dst=dst, th=th, ti=tbl_i, c=c):
                w(v); wt(v)
                v.stream_shuffle(shtmp[:, :], dst, [i ^ 1 for i in range(32)])
                v.tensor_mul(p1tmp[:, :], dst, tblb[ti][:, th * TH:(th + 1) * TH])
                v.tensor_mul(shtmp[:, :], shtmp[:, :],
                             tblb[ti][:, S + th * TH: S + (th + 1) * TH])
                v.tensor_add(dst, p1tmp[:, :], shtmp[:, :]).then_inc(sV.h, 1)
            P.emit("vector", f_v)
            nv = sV.inc()
            tbl_free_v[tbl_i] = nv

            def f_s2(s, w=wait(sV, nv), dst=dst, c=c, wA=wA):
                w(s); wA(s)
                s.mul(dst, dst, Acol[:, c:c + 1]).then_inc(sS.h, 1)
            P.emit("scalar", f_s2)
            return None, sS.inc(), nv
        else:
            def f_s(s, w=wait(sT, gcount), dst=dst, pt=ptile, bc=bcol, c=c, wA=wA):
                w(s); wA(s)
                s.activation(dst, pt, AF.Identity, bias=bc, scale=1.0)
                s.mul(dst, dst, Acol[:, c:c + 1]).then_inc(sS.h, 1)
            P.emit("scalar", f_s)
            ns = sS.inc()

            def f_v(v, w=wait(sS, ns)):
                w(v)
                v.engine_nop().then_inc(sV.h, 1)
            P.emit("vector", f_v)
            return None, ns, sV.inc()

    INV_BC = []
    vtmp_user = {0: 0, 1: 0}
    A_HOLD = []

    ep_free = {0: 0, 1: 0}
    ep_free_sem = {0: None, 1: None}
    g_idx = 0
    for i, ec in enumerate(qkv_order):
        bi = i % 2
        free_at = wbuf_users[bi]
        tgt = stripe_load(wqkv_h[ec, :, :], bi, free_at)
        stripe_tgts[ec] = tgt
        ti, ttgt = 0, 0
        if ec in tbl_plan:
            ti = tbl_plan[ec] % 2
            ttgt = tbl_load(tbl_plan[ec], ti, tbl_free_v[ti])

        for th in range(NTH):
            par = g_idx % 2
            off = par * TH
            ew = None
            if ep_free_sem[par] is not None:
                sem, val = ep_free_sem[par]
                ew = wait(sem, val)
            g = mm_group(off, bi, tgt, xnrhs(th), KC, extra_wait=ew)
            ptile = pM[:, off:off + TH]
            if ec == 32 and th == 0 and not INV_BC:
                pass
            gs, ns, nv = p1_epilogue(ec, th, ptile, g, ti, ttgt)
            ep_free_sem[par] = (sS, ns)
            g_idx += 1
        wbuf_users[bi] = sT.n

        if ec == 31:
            KV = sV.n
            kevs = emit_stats_pass(
                lambda c, tq: bufC[:, c * S + tq * 512: c * S + tq * 512 + 512],
                ksqt, False, sq_first=[wait(sV, KV)],
                mm_first=[wait(sV, ln1_bc)])
            nev = kevs[-1]
            def f_s(s, w=wait(sV, nev)):
                w(s)
                last = None
                for q in range(4):
                    last = s.activation(row(R_S1)[0:1, q * 512:(q + 1) * 512],
                                        row(R_S2)[0:1, q * 512:(q + 1) * 512], AF.Sqrt,
                                        bias=CT[0:1, 3:4], scale=1.0)
                last.then_inc(sS.h, 1)
            P.emit("scalar", f_s)
            nsq = sS.inc()

            def f_vi(v, w=wait(sS, nsq)):
                w(v)
                rts(v, row(R_S1), row(R_S1), 1e-12, None, OP.max)
                rrecip(v, row(R_INV), row(R_S1)).then_inc(sV.h, 1)
            P.emit("vector", f_vi)
            ninv = sV.inc()
            bcv = emit_bcast(invnb, R_INV, ninv, "invn")
            INV_BC.append(bcv)

        if ec == 47:
            def f_va(v, w=wait(sV, sV.n)):
                w(v)
                v.tensor_add(Acol[:, :], Apart.ap().rearrange("p (c two) -> p c two", two=2)[:, :, 0],
                             Apart.ap().rearrange("p (c two) -> p c two", two=2)[:, :, 1])
                v.tensor_mul(bAcol[:, :], Acol[:, :], VC[:, 0:16]).then_inc(sV.h, 1)
            P.emit("vector", f_va)
            A_HOLD.append(sV.inc())
            tbl_free_v[0] = sV.n
            tbl_free_v[1] = sV.n

            if "a" in dbg:
                def f_da(g, w=wait(sV, A_HOLD[0])):
                    w(g)
                    g.dma_start(out=dbg["a"][:, :], in_=Acol[:, :]).then_inc(sBulk.h, 16)
                P.emit("gpsimd", f_da)
                sBulk.inc(16)
            if "k" in dbg:
                def f_dk(g, w=wait(sV, sV.n)):
                    w(g)
                    g.dma_start(out=dbg["k"][:, :], in_=bufC[:, :]).then_inc(sBulk.h, 16)
                P.emit("gpsimd", f_dk)
                sBulk.inc(16)

    P1_T = sT.n
    P1_V = sV.n
    P1_S = sS.n

    if "q" in dbg:
        def f_dq(g, w=wait(sS, P1_S), w2=wait(sV, P1_V)):
            w(g); w2(g)
            g.dma_start(out=dbg["q"][:, :], in_=bufC[:, :]).then_inc(sBulk.h, 16)
        P.emit("gpsimd", f_dq)
        sBulk.inc(16)

    globals()["_ADDRS"] = {nm: nc.lookup_mloc(h).addr for nm, h in
                           [("VC", VC), ("CWB", CWB), ("CW", CW), ("sqt", sqt),
                            ("stb8", stb8), ("scb", scb), ("ARENA", OV.handle),
                            ("SC8", SC8), ("CT", CT), ("ones_c", ones_c),
                            ("rowbf", rowbf), ("tnfm", tnfm), ("thfm", thfm)]}
    extra = continue_build(nc, P, locals())

    with nc.Block() as block:
        @block.gpsimd
        def _(g):
            for fn in P.q["gpsimd"]:
                fn(g)

        @block.tensor
        def _(t):
            for fn in P.q["tensor"]:
                fn(t)

        @block.scalar
        def _(s):
            for fn in P.q["scalar"]:
                fn(s)

        @block.vector
        def _(v):
            for fn in P.q["vector"]:
                fn(v)

        @block.sync
        def _(sy):
            for fn in P.q["sync"]:
                fn(sy)
    return nc


def finish_stub(P, g):
    sV, sS, sT, sG = g["sV"], g["sS"], g["sT"], g["sG"]
    sOUT, sCC, sST = g["sOUT"], g["sCC"], g["sST"]
    yt_h, bufC = g["yt_h"], g["bufC"]
    sBulk = g["sBulk"]

    def f_fin(gp, wv=wait(sV, sV.n), ws=wait(sS, sS.n), wt=wait(sT, sT.n),
              wg=wait(sG, sG.n), wc=wait(sCC, sCC.n), wst=wait(sST, sST.n),
              wb=wait(sBulk, sBulk.n)):
        wv(gp); ws(gp); wt(gp); wg(gp); wc(gp); wst(gp); wb(gp)
        gp.dma_start(out=yt_h.rearrange("(c p) s -> p c s", p=128),
                     in_=bufC.ap().rearrange("p (c s) -> p c s", c=KC)).then_inc(sOUT.h, 16)
        gp.wait_ge(sOUT.h, 16)
    P.emit("gpsimd", f_fin)
    sOUT.inc(16)
    return {}


def continue_build(nc, P, L):
    g = dict(L)
    sT, sV, sS, sG = g["sT"], g["sV"], g["sS"], g["sG"]
    sWB, sCB, sTB, sFS = g["sWB"], g["sCB"], g["sTB"], g["sFS"]
    sSY, sSCR, sOUT, sST, sCC, sBulk = g["sSY"], g["sSCR"], g["sOUT"], g["sST"], g["sCC"], g["sBulk"]
    sX1T, sXI = g["sX1T"], g["sXI"]
    pM, pC, pT, pRow = g["pM"], g["pC"], g["pT"], g["pRow"]
    bufB, bufC = g["bufB"], g["bufC"]
    wbuf, cbuf = g["wbuf"], g["cbuf"]
    mu1b, r1b = g["mu1b"], g["r1b"]
    m2b, r2b, m3b, r3b = g["m2b"], g["r2b"], g["m3b"], g["r3b"]
    RW0, RW1, SSC, SC8 = g["RW0"], g["RW1"], g["SSC"], g["SC8"]
    VC, CT = g["VC"], g["CT"]
    row = g["row"]
    fstream, x1tmp, x1in, x2out = g["fstream"], g["x1tmp"], g["x1in"], g["x2out"]
    sigt, xsqt = g["sigt"], g["xsqt"]
    ylastc, Acol = g["ylastc"], g["Acol"]
    thfm, tnfm, ufm = g["thfm"], g["tnfm"], g["ufm"]
    CW, CWB, sqt, stb8, scb = g["CW"], g["CWB"], g["sqt"], g["stb8"], g["scb"]
    pngrep, pnbrep = g["pngrep"], g["pnbrep"]
    dlt, bcall, bcown = g["dlt"], g["bcall"], g["bcown"]
    selm, d2brep, d1bcol = g["selm"], g["d2brep"], g["d1bcol"]
    ones_c, ones_r, rowbf = g["ones_c"], g["ones_r"], g["rowbf"]
    wout_h, wgate_h, wagg_h, wbc_h, wd1_h, wd2_h = (
        g["wout_h"], g["wgate_h"], g["wagg_h"], g["wbc_h"], g["wd1_h"], g["wd2_h"])
    xt_h, yt_h, x1scr = g["xt_h"], g["yt_h"], g["x1scr"]
    xlastc_h = g["xlastc_h"]
    pngrep_h, pnbrep_h = g["pngrep_h"], g["pnbrep_h"]
    agin, agout, arin, arout = g["agin"], g["agout"], g["arin"], g["arout"]
    dbg = g["dbg"]
    emit_ln_rows, emit_bcast = g["emit_ln_rows"], g["emit_bcast"]
    stripe_load = g["stripe_load"]
    wbuf_users = g["wbuf_users"]
    R_S1, R_MU, R_M2, R_SD, R_S2, R_RS, R_E2, R_SP = (
        g["R_S1"], g["R_MU"], g["R_M2"], g["R_SD"], g["R_S2"], g["R_RS"],
        g["R_E2"], g["R_SP"])
    ln1_rows = g["ln1_rows"]
    P1_V, P1_S = g["P1_V"], g["P1_S"]
    RG = [list(range(8))]

    def vgap(v):
        v.tensor_copy(stb8[:, 26:28], VC[:, 0:2])

    def f_sy0(sy, w=wait(sV, P1_V)):
        w(sy)
        sy.dma_start(out=pngrep[:, :], in_=pngrep_h[:, :]).then_inc(sSY.h, 16)
        sy.dma_start(out=pnbrep[:, :], in_=pnbrep_h[:, :]).then_inc(sSY.h, 16)
    P.emit("sync", f_sy0)
    sSY.inc(32)
    SY0 = sSY.n

    def col_stats(src_cw, wprev, eps_col, sc0, md):
        def f_sc(s, w=wprev, src=src_cw):
            w(s)
            s.activation(sqt[:, 16:32], src, AF.Identity)
            s.activation(sqt[:, 0:16], src, AF.Square)
            s.activation(sqt[:, 64:128], CW[:, 32:96], AF.Square).then_inc(sS.h, 1)
        P.emit("scalar", f_sc)
        nsc = sS.inc()

        def f_mm(t, w=wait(sS, nsc)):
            w(t)
            t.matmul(pRow[0:1, 32:48], ones_c[:, 0:1], sqt[:, 16:32],
                     start=True, stop=True)
            t.matmul(pRow[0:1, 48:64], ones_c[:, 0:1], sqt[:, 0:16],
                     start=True, stop=True).then_inc(sT.h, 1)
        P.emit("tensor", f_mm)
        tmm2 = sT.inc()

        def f_rd(v, w=wait(sT, tmm2)):
            w(v)
            v.tensor_copy(SSC[0:1, 0:16], pRow[0:1, 32:48])
            v.tensor_copy(SSC[0:1, 16:32], pRow[0:1, 48:64])
            for wd in (8, 4, 2, 1):
                v.tensor_add(SSC[0:1, 0:wd], SSC[0:1, 0:wd], SSC[0:1, wd:2 * wd])
                v.tensor_add(SSC[0:1, 16:16 + wd], SSC[0:1, 16:16 + wd],
                             SSC[0:1, 16 + wd:16 + 2 * wd])
            v.tensor_scalar(SSC[0:1, 0:1], SSC[0:1, 0:1], 1.0 / D, None, OP.mult)
            v.tensor_scalar(SSC[0:1, 1:2], SSC[0:1, 16:17], 1.0 / D, None, OP.mult)
            v.tensor_mul(SSC[0:1, 2:3], SSC[0:1, 0:1], SSC[0:1, 0:1])
            vgap(v)
            v.tensor_sub(SSC[0:1, 1:2], SSC[0:1, 1:2], SSC[0:1, 2:3])
            vgap(v)
            v.engine_nop().then_inc(sV.h, 1)
        P.emit("vector", f_rd)
        nrd = sV.inc()

        def f_sq2(s, w=wait(sV, nrd), ec=eps_col):
            w(s)
            s.activation(SSC[0:1, 2:3], SSC[0:1, 1:2], AF.Sqrt,
                         bias=CT[0:1, ec:ec + 1], scale=1.0).then_inc(sS.h, 1)
        P.emit("scalar", f_sq2)
        nsq = sS.inc()

        def f_rc(v, w=wait(sS, nsq), sc=sc0):
            w(v)
            v.reciprocal(SSC[0:1, 3:4], SSC[0:1, 2:3])
            vgap(v)
            v.tensor_copy(scb[0:1, sc:sc + 1], SSC[0:1, 0:1])
            v.tensor_copy(scb[0:1, sc + 1:sc + 2], SSC[0:1, 3:4])
            v.tensor_copy(stb8[:, 24:32], VC[:, 0:8]).then_inc(sV.h, 1)
        P.emit("vector", f_rc)
        nrc = sV.inc()

        def f_bm(t, w=wait(sV, nrc), sc=sc0, md=md):
            w(t)
            t.matmul(pRow[:, md:md + 2], ones_r[0:1, :], scb[0:1, sc:sc + 2],
                     start=True, stop=True).then_inc(sT.h, 1)
        P.emit("tensor", f_bm)
        nbm = sT.inc()

        def f_be(v, w=wait(sT, nbm), md=md):
            w(v)
            v.tensor_copy(stb8[:, md:md + 2], pRow[:, md:md + 2])
            vgap(v)
            v.engine_nop().then_inc(sV.h, 1)
        P.emit("vector", f_be)
        return sV.inc()

    cb_users = [0, 0]

    def cstripe(src_ap, bi, width):
        sCB[bi].inc(16)
        tgt = sCB[bi].n

        def f_w(g2, src=src_ap, bi=bi, need=cb_users[bi]):
            if need > 0:
                g2.wait_ge(sT.h, need)
            g2.dma_start(out=cbuf[bi][:, 0:width], in_=src).then_inc(sCB[bi].h, 16)
        P.emit("gpsimd", f_w)
        return tgt

    def chain_proj(w_h_arr, n_out, nk, rhs_ap_fn, evac_fn, first_wait=None,
                   kwidth=KC * 128):
        ev_gate = {0: None, 1: None}
        for ci in range(n_out):
            bi = ci % 2
            tgt = cstripe(w_h_arr[ci, :, :], bi, kwidth)

            def f_t(t, ci=ci, bi=bi, tgt=tgt, nk=nk,
                    w=first_wait if ci == 0 else None, wev=ev_gate[ci % 2]):
                t.wait_ge(sCB[bi].h, tgt)
                if w is not None:
                    w(t)
                if wev is not None:
                    wev(t)
                pa = (pC if ci % 2 == 0 else pRow)[:, (ci // 2) * 8:(ci // 2) * 8 + 8]
                last = None
                for k in range(nk):
                    last = t.matmul(pa, cbuf[bi][:, k * 128:(k + 1) * 128],
                                    rhs_ap_fn(k), start=(k == 0), stop=(k == nk - 1))
                last.then_inc(sT.h, 1)
            P.emit("tensor", f_t)
            nt = sT.inc()
            cb_users[bi] = nt
            sem, cnt = evac_fn(ci, (pC if ci % 2 == 0 else pRow)
                               [:, (ci // 2) * 8:(ci // 2) * 8 + 8], nt)
            ev_gate[ci % 2] = wait(sem, cnt)


    fs_user = [0, 0]
    P2_free = {0: wait(sS, P1_S), 1: wait(sS, P1_S)}
    gi = [0]
    for ec in range(KC):
        fi = ec % 2
        sFS[fi].inc(16)
        fst = sFS[fi].n
        if ec >= 2:
            def f_fx(sy, ec=ec, fi=fi, need=fs_user[fi]):
                sy.wait_ge(sV.h, need)
                sy.dma_start(out=fstream[fi][:, :],
                             in_=xt_h[ec * 128:(ec + 1) * 128, :]).then_inc(sFS[fi].h, 16)
        else:
            def f_fx(sy, ec=ec, fi=fi, wp=wait(sV, P1_V)):
                wp(sy)
                sy.dma_start(out=fstream[fi][:, :],
                             in_=xt_h[ec * 128:(ec + 1) * 128, :]).then_inc(sFS[fi].h, 16)
        P.emit("sync", f_fx)

        bi = ec % 2
        tgt = stripe_load(wout_h[ec, :, :], bi, wbuf_users[bi])
        for th in range(NTH):
            par = gi[0] % 2
            off = par * TH

            def f_t(t, ec=ec, th=th, bi=bi, tgt=tgt, off=off,
                    wfree=P2_free[par]):
                t.wait_ge(sWB[bi].h, tgt)
                if wfree is not None:
                    wfree(t)
                last = None
                for k in range(KC):
                    for h in range(2):
                        lo = k * S + th * TH + h * 512
                        last = t.matmul(pM[:, off + h * 512: off + h * 512 + 512],
                                        wbuf[bi][:, k * 128:(k + 1) * 128],
                                        bufC[:, lo: lo + 512],
                                        start=(k == 0), stop=(k == KC - 1))
                last.then_inc(sT.h, 1)
            P.emit("tensor", f_t)
            nt = sT.inc()
            gi[0] += 1

            def f_s1(s, ec=ec, th=th, nt=nt, off=off,
                     wret=(wait(sX1T[th], 16 * ec) if ec > 0 else None)):
                s.wait_ge(sT.h, nt)
                if wret is not None:
                    wret(s)
                s.activation(x1tmp[:, th * TH:(th + 1) * TH], pM[:, off:off + TH],
                             AF.Identity, bias=VC[:, 48 + ec:49 + ec],
                             scale=1.0).then_inc(sS.h, 1)
            P.emit("scalar", f_s1)
            ns1 = sS.inc()
            P2_free[par] = wait(sS, ns1)

            def f_v1(v, th=th, fi=fi, fst=fst, ns1=ns1):
                v.wait_ge(sFS[fi].h, fst)
                v.wait_ge(sS.h, ns1)
                v.tensor_add(x1tmp[:, th * TH:(th + 1) * TH],
                             x1tmp[:, th * TH:(th + 1) * TH],
                             fstream[fi][:, th * TH:(th + 1) * TH]).then_inc(sV.h, 1)
            P.emit("vector", f_v1)
            nv1 = sV.inc()

            def f_g1(g2, th=th, fi=fi, nv1=nv1):
                g2.wait_ge(sV.h, nv1)
                g2.tensor_sub(fstream[fi][:, th * TH:(th + 1) * TH],
                              fstream[fi][:, th * TH:(th + 1) * TH],
                              mu1b[:, th * TH:(th + 1) * TH])
                g2.tensor_mul(fstream[fi][:, th * TH:(th + 1) * TH],
                              fstream[fi][:, th * TH:(th + 1) * TH],
                              r1b[:, th * TH:(th + 1) * TH]).then_inc(sG.h, 1)
            P.emit("gpsimd", f_g1)
            ng1 = sG.inc()

            def f_v2(v, th=th, fi=fi, ng1=ng1):
                v.wait_ge(sG.h, ng1)
                v.tensor_add(x1tmp[:, th * TH:(th + 1) * TH],
                             x1tmp[:, th * TH:(th + 1) * TH],
                             fstream[fi][:, th * TH:(th + 1) * TH]).then_inc(sV.h, 1)
            P.emit("vector", f_v2)
            nv2 = sV.inc()
            fs_user[fi] = nv2

            def f_s2(s, ec=ec, th=th, nv2=nv2):
                s.wait_ge(sV.h, nv2)
                s.activation(bufB[:, ec * S + th * TH: ec * S + (th + 1) * TH],
                             x1tmp[:, th * TH:(th + 1) * TH],
                             AF.Identity).then_inc(sS.h, 1)
            P.emit("scalar", f_s2)
            ns2 = sS.inc()

            def f_sy(sy, ec=ec, th=th, nv2=nv2):
                sy.wait_ge(sV.h, nv2)
                sy.dma_start(out=x1scr[ec * 128:(ec + 1) * 128, th * TH:(th + 1) * TH],
                             in_=x1tmp[:, th * TH:(th + 1) * TH]).then_inc(sX1T[th].h, 16)
            P.emit("sync", f_sy)
            sX1T[th].inc(16)
        wbuf_users[bi] = sT.n
    P2_V = sV.n
    P2_S = sS.n
    P2_T = sT.n

    if "x1" in dbg:
        def f_dx1(g2, w=wait(sS, P2_S)):
            w(g2)
            g2.dma_start(out=dbg["x1"][:, :], in_=bufB[:, :]).then_inc(sBulk.h, 16)
        P.emit("gpsimd", f_dx1)
        sBulk.inc(16)

    if STOP == "p2":
        return finish_stub(P, g)
    emit_stats_pass = g["emit_stats_pass"]

    def emit_full_stats(src_buf, sq_bufs, prev_done_s, prev_done_v, eps_col):
        evs = emit_stats_pass(
            lambda c, tq: src_buf[:, c * S + tq * 512: c * S + tq * 512 + 512],
            sq_bufs, True,
            sq_first=[wait(sS, prev_done_s), wait(sV, prev_done_v)],
            mm_first=[wait(sV, prev_done_v)])
        return emit_ln_rows(None, eps_col, R_MU, R_RS, evs[-1])

    ln2_rows = emit_full_stats(bufB, xsqt, P2_S, P2_V, 0)
    ln2_bc = emit_bcast(m2b, R_MU, ln2_rows, "mu2")
    ln2_bc = emit_bcast(r2b, R_RS, ln2_bc, "r2")

    p25 = []
    for c in range(KC):
        def f_n(v, c=c, w=wait(sV, ln2_bc) if c == 0 else None):
            if w is not None:
                w(v)
            v.tensor_sub(bufB[:, c * S:(c + 1) * S], bufB[:, c * S:(c + 1) * S],
                         m2b[:, :])
            v.tensor_mul(bufB[:, c * S:(c + 1) * S], bufB[:, c * S:(c + 1) * S],
                         r2b[:, :]).then_inc(sV.h, 1)
        P.emit("vector", f_n)
        p25.append(sV.inc())
    P25_V = p25[-1]

    def f_xl(v, w=wait(sV, P25_V)):
        w(v)
        v.tensor_copy(CW[:, 32:48],
                      bufB.ap().rearrange("p (c s) -> p c s", c=KC)[:, :, S - 1])
        vgap(v)
        v.engine_nop().then_inc(sV.h, 1)
    P.emit("vector", f_xl)
    nxl = sV.inc()

    st2 = col_stats(CW[:, 32:48], wait(sV, nxl), 1, 4, 4)

    def f_pe8(v, w=wait(sV, st2)):
        w(v)
        v.tensor_scalar(CW[:, 48:64], CW[:, 32:48], stb8[:, 4:5], stb8[:, 5:6],
                        OP.subtract, OP.mult)
        vgap(v)
        v.tensor_mul(CW[:, 48:64], CW[:, 48:64], VC[:, 176:192])
        vgap(v)
        v.tensor_add(CW[:, 48:64], CW[:, 48:64], VC[:, 192:208])
        vgap(v)
        v.engine_nop().then_inc(sV.h, 1)
    P.emit("vector", f_pe8)
    XNL_COL = sV.inc()

    if "xnl" in dbg:
        def f_dxnl(g2, w=wait(sV, XNL_COL)):
            w(g2)
            g2.dma_start(out=dbg["xnl"][:, :], in_=CW[:, 48:64]).then_inc(sBulk.h, 16)
        P.emit("gpsimd", f_dxnl)
        sBulk.inc(16)
    if "rows" in dbg:
        def f_drw(g2, w=wait(sV, XNL_COL)):
            w(g2)
            g2.dma_start(out=dbg["rows"][0, :, 0:96], in_=CW[:, :]).then_inc(sBulk.h, 16)
            g2.dma_start(out=dbg["rows"][0, :, 96:128], in_=stb8[:, :]).then_inc(sBulk.h, 16)
            g2.dma_start(out=dbg["rows"][1, 0:16, 0:32], in_=SC8[:, :]).then_inc(sBulk.h, 16)
            g2.dma_start(out=dbg["rows"][1, :, 100:116], in_=sqt[:, 16:32]).then_inc(sBulk.h, 16)
            g2.dma_start(out=dbg["rows"][1, :, 50:66], in_=sqt[:, 0:16]).then_inc(sBulk.h, 16)
            g2.dma_start(out=dbg["rows"][1, :, 120:248], in_=sqt[:, :]).then_inc(sBulk.h, 16)
        P.emit("gpsimd", f_drw)
        sBulk.inc(96)

    def f_ag1(g2, w=wait(sV, XNL_COL)):
        w(g2)
        g2.dma_start(out=agin[:, :], in_=CW[:, 48:64]).then_inc(sST.h, 16)
    P.emit("gpsimd", f_ag1)
    sST.inc(16); ag1 = sST.n

    def f_ag2(g2, w=wait(sST, ag1)):
        w(g2)
        g2.collective_compute("AllGather", OP.bypass, ins=[agin[:, :]],
                              outs=[agout[:, :]], replica_groups=RG).then_inc(sCC.h, 1)
    P.emit("gpsimd", f_ag2)
    agd = sCC.inc()

    def f_ag3(s, w=wait(sCC, agd)):
        w(s)
        s.dma_start(out=sqt.ap().rearrange("p (r c) -> p r c", r=8),
                    in_=agout.rearrange("(r p) c -> p r c", p=128)).then_inc(sST.h, 16)
    P.emit("scalar", f_ag3)
    sST.inc(16); ag3 = sST.n

    if STOP == "p2e":
        return finish_stub(P, g)
    if STOP == "ag":
        return finish_stub(P, g)
    def f_xe(v, w=wait(sST, ag3)):
        w(v)
        dst = tnfm.ap().rearrange("p (c r) -> p c r", r=8)
        for r in range(8):
            v.tensor_copy(dst[:, :, r], sqt[:, r * 16:(r + 1) * 16])
        vgap(v)
        v.engine_nop().then_inc(sV.h, 1)
    P.emit("vector", f_xe)
    XNLFM = sV.inc()

    agg_ev = []

    def agg_evac(ci, psum_ap, nt):
        def f_s(s, ci=ci, pa=psum_ap, w=wait(sT, nt)):
            w(s)
            s.activation(thfm[:, ci * 8:(ci + 1) * 8], pa, AF.Identity,
                         bias=VC[:, 80 + ci:81 + ci], scale=1.0).then_inc(sS.h, 1)
        P.emit("scalar", f_s)
        agg_ev.append(sS.inc())
        return sS, agg_ev[-1]

    chain_proj(wagg_h, KC, KC, lambda k: tnfm[:, k * 8:(k + 1) * 8],
               agg_evac, first_wait=wait(sV, XNLFM))
    TH0_S = agg_ev[-1]

    if "th" in dbg:
        def f_dth0(g2, w=wait(sS, TH0_S)):
            w(g2)
            g2.dma_start(out=dbg["th"][0, :, 0:128], in_=thfm[:, :]).then_inc(sBulk.h, 16)
        P.emit("gpsimd", f_dth0)
        sBulk.inc(16)

    if STOP == "agg":
        if "th" in dbg:
            def f_dthl(g2, w=wait(sS, TH0_S)):
                w(g2)
                g2.dma_start(out=dbg["th"][1, :, 0:128], in_=thfm[:, :]).then_inc(sBulk.h, 16)
                g2.dma_start(out=dbg["th"][2, :, 0:128], in_=tnfm[:, :]).then_inc(sBulk.h, 16)
                g2.dma_start(out=dbg["th"][3, :, 0:128], in_=sqt[:, :]).then_inc(sBulk.h, 16)
                g2.dma_start(out=dbg["th"][4, :, 0:16], in_=CW[:, 48:64]).then_inc(sBulk.h, 16)
            P.emit("gpsimd", f_dthl)
            sBulk.inc(64)
        return finish_stub(P, g)

    if STOP == "th0":
        return finish_stub(P, g)
    def chain_iteration(it):
        def f_cs1(v, w=wait(sS, TH0_S) if it == 0 else None):
            if w is not None:
                w(v)
            v.tensor_copy(tnfm[:, :], thfm[:, :])
            vgap(v)
            v.engine_nop().then_inc(sV.h, 1)
        P.emit("vector", f_cs1)
        c1 = sV.inc()
        if STOP == "cs0" and it == 0:
            return None

        def f_cm1(t, w=wait(sV, c1),
                  wln=wait(sV, ln2_bc) if it == 0 else None):
            w(t)
            if wln is not None:
                wln(t)
            t.matmul(pRow[0:1, 0:128], ones_c[:, 0:1], tnfm[:, :],
                     start=True, stop=True).then_inc(sT.h, 1)
        P.emit("tensor", f_cm1)
        sT.inc()
        if STOP == "cs1" and it == 0:
            return None

        def f_cs2(s, w=wait(sV, c1)):
            w(s)
            s.activation(sqt[:, :], thfm[:, :], AF.Square).then_inc(sS.h, 1)
        P.emit("scalar", f_cs2)
        s2 = sS.inc()

        def f_cm2(t, w=wait(sS, s2)):
            w(t)
            t.matmul(pRow[0:1, 128:256], ones_c[:, 0:1], sqt[:, :],
                     start=True, stop=True).then_inc(sT.h, 1)
        P.emit("tensor", f_cm2)
        t2 = sT.inc()

        def f_cv2(v, w=wait(sT, t2)):
            w(v)
            v.tensor_copy(RW0[0:1, 0:256], pRow[0:1, 0:256])
            for wd in (64, 32, 16, 8):
                v.tensor_add(RW0[0:1, 0:wd], RW0[0:1, 0:wd], RW0[0:1, wd:2 * wd])
                v.tensor_add(RW0[0:1, 128:128 + wd], RW0[0:1, 128:128 + wd],
                             RW0[0:1, 128 + wd:128 + 2 * wd])
            v.tensor_scalar(SSC[0:1, 0:8], RW0[0:1, 0:8], 1.0 / D, None, OP.mult)
            v.tensor_scalar(SSC[0:1, 8:16], RW0[0:1, 128:136], 1.0 / D, None, OP.mult)
            v.tensor_mul(SSC[0:1, 16:24], SSC[0:1, 0:8], SSC[0:1, 0:8])
            vgap(v)
            v.tensor_sub(SSC[0:1, 8:16], SSC[0:1, 8:16], SSC[0:1, 16:24])
            vgap(v)
            v.engine_nop().then_inc(sV.h, 1)
        P.emit("vector", f_cv2)
        c2v = sV.inc()
        if STOP == "cs2" and it == 0:
            return None

        def f_cs3(s, w=wait(sV, c2v)):
            w(s)
            s.activation(SSC[0:1, 16:24], SSC[0:1, 8:16], AF.Sqrt,
                         bias=CT[0:1, 1:2], scale=1.0).then_inc(sS.h, 1)
        P.emit("scalar", f_cs3)
        c3 = sS.inc()
        if STOP == "csA" and it == 0:
            return None

        def f_cv3(v, w=wait(sS, c3)):
            w(v)
            v.reciprocal(SSC[0:1, 24:32], SSC[0:1, 16:24])
            vgap(v)
            v.engine_nop().then_inc(sV.h, 1)
        P.emit("vector", f_cv3)
        c3v0 = sV.inc()

        def f_cv3b(s, w=wait(sV, c3v0)):
            w(s)
            s.activation(scb[0:1, 8:16], SSC[0:1, 0:8], AF.Identity)
            s.activation(scb[0:1, 16:24], SSC[0:1, 24:32], AF.Identity)
            s.activation(sqt[:, 64:128], CW[:, 32:96], AF.Square).then_inc(sS.h, 1)
        P.emit("scalar", f_cv3b)
        c3v = sS.inc()
        if STOP == "cs" and it == 0:
            return None

        def f_cm3(t, w=wait(sS, c3v)):
            w(t)
            t.matmul(pRow[:, 8:24], ones_r[0:1, :], scb[0:1, 8:24],
                     start=True, stop=True).then_inc(sT.h, 1)
        P.emit("tensor", f_cm3)
        t3 = sT.inc()

        def f_cv4(v, w=wait(sT, t3),
                  wsy=wait(sSY, SY0) if it == 0 else None):
            w(v)
            if wsy is not None:
                wsy(v)
            v.tensor_copy(stb8[:, 8:24], pRow[:, 8:24])
            vgap(v)
            dst = tnfm.ap().rearrange("p (c r) -> p c r", r=8)
            srcv = thfm.ap().rearrange("p (c r) -> p c r", r=8)
            for r in range(8):
                v.tensor_scalar(dst[:, :, r], srcv[:, :, r], stb8[:, 8 + r:9 + r],
                                stb8[:, 16 + r:17 + r], OP.subtract, OP.mult)
            vgap(v)
            v.tensor_mul(tnfm[:, :], tnfm[:, :], pngrep[:, :])
            v.tensor_add(tnfm[:, :], tnfm[:, :], pnbrep[:, :])
            vgap(v)
            v.engine_nop().then_inc(sV.h, 1)
        P.emit("vector", f_cv4)
        nte = sV.inc()
        if STOP == "cv4" and it == 0:
            return None

        d1_ev = []

        def d1_evac(ci, pa, nt):
            def f_s(s, ci=ci, pa=pa, w=wait(sT, nt)):
                w(s)
                s.activation(ufm[:, ci * 8:(ci + 1) * 8], pa, AF.Gelu,
                             bias=d1bcol[:, ci:ci + 1], scale=1.0).then_inc(sS.h, 1)
            P.emit("scalar", f_s)
            d1_ev.append(sS.inc())
            return sS, d1_ev[-1]
        chain_proj(wd1_h, HC, KC, lambda k: tnfm[:, k * 8:(k + 1) * 8],
                   d1_evac, first_wait=wait(sV, nte))
        if STOP == "d1" and it == 0:
            return None

        d2_ev = []

        def d2_evac(ci, pa, nt):
            def f_v(v, ci=ci, pa=pa, w=wait(sT, nt)):
                w(v)
                v.tensor_copy(dlt[:, ci * 8:(ci + 1) * 8], pa).then_inc(sV.h, 1)
            P.emit("vector", f_v)
            d2_ev.append(sV.inc())
            return sV, d2_ev[-1]
        chain_proj(wd2_h, KC, HC, lambda k: ufm[:, k * 8:(k + 1) * 8],
                   d2_evac, first_wait=wait(sS, d1_ev[-1]) if d1_ev else None,
                   kwidth=HC * 128)

        if STOP == "ar0" and it == 0:
            if "ch0" in dbg:
                def f_dpre(g2, w=wait(sV, d2_ev[-1])):
                    w(g2)
                    g2.dma_start(out=dbg["ch0"][0, :, 0:128], in_=tnfm[:, :]).then_inc(sBulk.h, 16)
                    g2.dma_start(out=dbg["ch0"][1, :, 0:64], in_=ufm[:, :]).then_inc(sBulk.h, 16)
                    g2.dma_start(out=dbg["ch0"][2, :, 0:128], in_=dlt[:, :]).then_inc(sBulk.h, 16)
                    g2.dma_start(out=dbg["ch0"][3, :, 0:128], in_=sqt[:, :]).then_inc(sBulk.h, 16)
                    g2.dma_start(out=dbg["ch0"][0, :, 256:384], in_=thfm[:, :]).then_inc(sBulk.h, 16)
                    g2.dma_start(out=dbg["ch0"][1, :, 256:288], in_=stb8[:, :]).then_inc(sBulk.h, 16)
                    g2.dma_start(out=dbg["ch0"][2, 0:16, 256:288], in_=SC8[:, :]).then_inc(sBulk.h, 16)
                P.emit("gpsimd", f_dpre)
                sBulk.inc(112)
            return None
        def f_r1(s, w=wait(sV, d2_ev[-1]), it=it):
            w(s)
            s.dma_start(out=arin[it][:, :], in_=dlt[:, :]).then_inc(sST.h, 16)
        P.emit("scalar", f_r1)
        sST.inc(16); r1 = sST.n

        def f_r2(g2, w=wait(sST, r1), it=it):
            w(g2)
            g2.collective_compute("AllReduce", OP.add, ins=[arin[it][:, :]],
                                  outs=[arout[it][:, :]],
                                  replica_groups=RG).then_inc(sCC.h, 1)
        P.emit("gpsimd", f_r2)
        r2n = sCC.inc()

        def f_r3(s, w=wait(sCC, r2n), it=it):
            w(s)
            s.dma_start(out=dlt[:, :], in_=arout[it][:, :]).then_inc(sST.h, 16)
        P.emit("scalar", f_r3)
        sST.inc(16); r3 = sST.n

        def f_up(v, w=wait(sST, r3)):
            w(v)
            v.tensor_add(thfm[:, :], thfm[:, :], dlt[:, :])
            v.tensor_add(thfm[:, :], thfm[:, :], d2brep[:, :])
            vgap(v)
            v.engine_nop().then_inc(sV.h, 1)
        P.emit("vector", f_up)
        nup = sV.inc()

        if "th" in dbg:
            def f_dth(g2, w=wait(sV, nup), it=it):
                w(g2)
                g2.dma_start(out=dbg["th"][it + 1, :, 0:128], in_=thfm[:, :]).then_inc(sBulk.h, 16)
            P.emit("gpsimd", f_dth)
            sBulk.inc(16)
        if "ch0" in dbg and it == 0:
            def f_dch(g2, w=wait(sV, nup)):
                w(g2)
                g2.dma_start(out=dbg["ch0"][0, :, 0:128], in_=tnfm[:, :]).then_inc(sBulk.h, 16)
                g2.dma_start(out=dbg["ch0"][1, :, 0:64], in_=ufm[:, :]).then_inc(sBulk.h, 16)
                g2.dma_start(out=dbg["ch0"][2, :, 0:128], in_=dlt[:, :]).then_inc(sBulk.h, 16)
                g2.dma_start(out=dbg["ch0"][3, :, 0:128], in_=sqt[:, :]).then_inc(sBulk.h, 16)
            P.emit("gpsimd", f_dch)
            sBulk.inc(64)
        return nup

    nup = 0
    for it in range(4):
        nup = chain_iteration(it)
        if STOP in ("ar0", "cs", "cv4", "d1", "cs0", "cs1", "cs2", "csA"):
            return finish_stub(P, g)

    def f_cbc(v, w=wait(sV, nup)):
        w(v)
        v.tensor_copy(tnfm[:, :], thfm[:, :])
        vgap(v)
        v.engine_nop().then_inc(sV.h, 1)
    P.emit("vector", f_cbc)
    ncbc = sV.inc()
    bc_ev = []

    def bc_evac(ci, pa, nt):
        def f_s(s, ci=ci, pa=pa, w=wait(sT, nt)):
            w(s)
            s.activation(bcall[:, ci * 8:(ci + 1) * 8], pa, AF.Identity,
                         bias=VC[:, 96 + ci:97 + ci], scale=1.0).then_inc(sS.h, 1)
        P.emit("scalar", f_s)
        bc_ev.append(sS.inc())
        return sS, bc_ev[-1]
    chain_proj(wbc_h, KC, KC, lambda k: tnfm[:, k * 8:(k + 1) * 8],
               bc_evac, first_wait=wait(sV, ncbc))

    def f_sel(v, w=wait(sS, bc_ev[-1])):
        w(v)
        v.tensor_mul(tnfm[:, :], bcall[:, :], selm[:, :])
        v.tensor_reduce(bcown[:, :],
                        tnfm.ap().rearrange("p (c r) -> p c r", r=8),
                        mybir.AxisListType.X, OP.add)
        vgap(v)
        v.engine_nop().then_inc(sV.h, 1)
    P.emit("vector", f_sel)
    BC_OWN = sV.inc()

    if "bc" in dbg:
        def f_dbc(g2, w=wait(sV, BC_OWN)):
            w(g2)
            g2.dma_start(out=dbg["bc"][:, :], in_=bcown[:, :]).then_inc(sBulk.h, 16)
        P.emit("gpsimd", f_dbc)
        sBulk.inc(16)

    if STOP == "chain":
        return finish_stub(P, g)
    P3_free = {0: None, 1: None}
    sig_user = {0: 0, 1: 0}
    gi3 = [0]
    out_done = []
    for ec in range(KC):
        bi = ec % 2
        tgt = stripe_load(wgate_h[ec, :, :], bi, wbuf_users[bi])
        for th in range(NTH):
            par = gi3[0] % 2
            off = par * TH

            def f_t(t, ec=ec, th=th, bi=bi, tgt=tgt, off=off,
                    wfree=P3_free[par],
                    wx=wait(sV, P25_V) if (ec == 0 and th == 0) else None):
                t.wait_ge(sWB[bi].h, tgt)
                if wfree is not None:
                    wfree(t)
                if wx is not None:
                    wx(t)
                last = None
                for k in range(KC):
                    for h in range(2):
                        lo = k * S + th * TH + h * 512
                        last = t.matmul(pM[:, off + h * 512: off + h * 512 + 512],
                                        wbuf[bi][:, k * 128:(k + 1) * 128],
                                        bufB[:, lo: lo + 512],
                                        start=(k == 0), stop=(k == KC - 1))
                last.then_inc(sT.h, 1)
            P.emit("tensor", f_t)
            nt = sT.inc()
            gi3[0] += 1
            slot = gi3[0] % 2

            def f_s(s, ec=ec, th=th, nt=nt, off=off, slot=slot,
                    wprev=(wait(sV, sig_user[slot]) if sig_user[slot] else None)):
                s.wait_ge(sT.h, nt)
                if wprev is not None:
                    wprev(s)
                s.activation(sigt[slot][:, :], pM[:, off:off + TH], AF.Sigmoid,
                             bias=VC[:, 64 + ec:65 + ec], scale=1.0).then_inc(sS.h, 1)
            P.emit("scalar", f_s)
            ns = sS.inc()
            P3_free[par] = wait(sS, ns)

            def f_v(v, ec=ec, th=th, ns=ns, slot=slot,
                    wbc=wait(sV, BC_OWN) if gi3[0] == 1 else None):
                v.wait_ge(sS.h, ns)
                if wbc is not None:
                    wbc(v)
                v.tensor_scalar(sigt[slot][:, :], sigt[slot][:, :],
                                bcown[:, ec:ec + 1], None, OP.mult)
                v.tensor_add(bufC[:, ec * S + th * TH: ec * S + (th + 1) * TH],
                             sigt[slot][:, :],
                             bufB[:, ec * S + th * TH: ec * S + (th + 1) * TH]).then_inc(sV.h, 1)
            P.emit("vector", f_v)
            nv = sV.inc()
            sig_user[slot] = nv
            out_done.append(nv)
        wbuf_users[bi] = sT.n
    P3_V = sV.n
    P3_S = sS.n

    if "out" in dbg:
        def f_dout(g2, w=wait(sV, P3_V)):
            w(g2)
            g2.dma_start(out=dbg["out"][:, :], in_=bufC[:, :]).then_inc(sBulk.h, 16)
        P.emit("gpsimd", f_dout)
        sBulk.inc(16)

    if STOP == "p3":
        return finish_stub(P, g)
    ln3_rows = emit_full_stats(bufC, xsqt, P3_S, P3_V, 1)
    ln3_bc = emit_bcast(m3b, R_MU, ln3_rows, "mu3")
    ln3_bc = emit_bcast(r3b, R_RS, ln3_bc, "r3")

    xi_user = {0: 0, 1: 0}
    o_writes = {0: 0, 1: 0}
    o_sem = {0: sOUT, 1: sSCR}
    gi4 = [0]
    for ec in range(KC):
        for th in range(NTH):
            xb = gi4[0] % 2
            sXI[xb].inc(16)
            xt = sXI[xb].n

            def f_x(sy, ec=ec, th=th, xb=xb,
                    need=xi_user[xb], wst=wait(sV, ln3_bc) if gi4[0] == 0 else None):
                if wst is not None:
                    wst(sy)
                if need > 0:
                    sy.wait_ge(sV.h, need)
                sy.dma_start(out=x1in[xb][:, :],
                             in_=x1scr[ec * 128:(ec + 1) * 128,
                                       th * TH:(th + 1) * TH]).then_inc(sXI[xb].h, 16)
            P.emit("sync", f_x)

            ob = gi4[0] % 2

            def f_g(g2, ec=ec, th=th, ob=ob,
                    wo=(wait(o_sem[ob], o_writes[ob]) if o_writes[ob] else None),
                    wst=wait(sV, ln3_bc) if gi4[0] < 2 else None):
                if wst is not None:
                    wst(g2)
                if wo is not None:
                    wo(g2)
                g2.tensor_sub(x2out[ob][:, :],
                              bufC[:, ec * S + th * TH: ec * S + (th + 1) * TH],
                              m3b[:, th * TH:(th + 1) * TH])
                g2.tensor_mul(x2out[ob][:, :], x2out[ob][:, :],
                              r3b[:, th * TH:(th + 1) * TH]).then_inc(sG.h, 1)
            P.emit("gpsimd", f_g)
            ngp = sG.inc()

            def f_v(v, ec=ec, th=th, ob=ob, xb=xb, ngp=ngp, xt=xt):
                v.wait_ge(sG.h, ngp)
                v.tensor_scalar(x2out[ob][:, :], x2out[ob][:, :],
                                VC[:, 144 + ec:145 + ec], VC[:, 160 + ec:161 + ec],
                                OP.mult, OP.add)
                v.wait_ge(sXI[xb].h, xt)
                v.tensor_add(x2out[ob][:, :], x2out[ob][:, :],
                             x1in[xb][:, :]).then_inc(sV.h, 1)
            P.emit("vector", f_v)
            nv4 = sV.inc()
            xi_user[xb] = nv4

            def f_o(sy, ec=ec, th=th, ob=ob, nv4=nv4):
                sy.wait_ge(sV.h, nv4)
                sy.dma_start(out=yt_h[ec * 128:(ec + 1) * 128, th * TH:(th + 1) * TH],
                             in_=x2out[ob][:, :]).then_inc(o_sem[ob].h, 16)
            P.emit("sync", f_o)
            o_sem[ob].inc(16)
            o_writes[ob] = o_sem[ob].n
            gi4[0] += 1

    def f_done(sy, wa=wait(sOUT, sOUT.n), wb=wait(sSCR, sSCR.n)):
        wa(sy); wb(sy)
    P.emit("sync", f_done)
    return {}



def kernel(**inputs):
    in_maps = host_prep(inputs)
    nc = build_nc()
    res = run_bass_kernel_spmd(nc, in_maps, core_ids=list(range(8)))
    out = np.empty((B, S, D), np.float32)
    for c in range(8):
        out[c] = np.ascontiguousarray(res.results[c]["yt"]).T
    return out
